# version 04
# baseline (speedup 1.0000x reference)
import numpy as np
import ml_dtypes
from contextlib import ExitStack

import concourse.bass as bass
import concourse.mybir as mybir
from concourse.bass_utils import run_bass_kernel_spmd

F32 = mybir.dt.float32
BF16 = mybir.dt.bfloat16
I32 = mybir.dt.int32
AF = mybir.ActivationFunctionType
ALU = mybir.AluOpType
AX = mybir.AxisListType

T = 2048
D = 1024
NKC = D // 128
LAT = 512
NLC = LAT // 128
H = 16
HD = 64
FF = 4096
NFC = FF // 128
SEG = 512
NSEG = T // SEG
NT = T // 128
KB = 1024

ENGS = ("sync", "tensor", "vector", "scalar", "gpsimd")


class Builder:
    def __init__(self, debug=False):
        self.debug = debug
        self.nc = bass.Bass("TRN2", target_bir_lowering=False,
                            detect_race_conditions=False)
        self.q = {e: [] for e in ENGS}
        self.cnt = {}
        self.handles = {}
        for s in ("dx", "dw", "dwf", "dscr", "dres", "dffn", "dout",
                  "pe", "act", "dve"):
            self.cnt[s] = 0

    def emit(self, eng, fn):
        self.q[eng].append(fn)

    def inc(self, eng, sem, n, build):
        self.cnt[sem] += n

        def f(e, build=build, sem=sem, n=n):
            build(e).then_inc(self.handles[sem], n)

        self.q[eng].append(f)
        return self.cnt[sem]

    def wait(self, eng, sem, val=None):
        v = self.cnt[sem] if val is None else val
        if v <= 0:
            return

        def f(e, sem=sem, v=v):
            e.wait_ge(self.handles[sem], v)

        self.q[eng].append(f)

    def ck(self, sem):
        return self.cnt[sem]

    def mm(self, out, lhsT, rhs, start=True, stop=True, tp=None):
        def f(e, out=out, lhsT=lhsT, rhs=rhs, start=start, stop=stop, tp=tp):
            return e.matmul(out, lhsT, rhs, start=start, stop=stop,
                            tile_position=tp, skip_group_check=True)
        return f

    def alloc(self):
        nc = self.nc
        self.din = {}
        for name, shape, dt in [
            ("xT", [D, T], F32),
            ("wcd", [D, LAT], F32),
            ("wqup", [LAT, D], F32),
            ("wkv", [LAT, 2 * HD], F32),
            ("wcproj", [D, D], F32),
            ("wfcT", [D, FF], F32),
            ("wpjT", [FF, D], F32),
            ("afc", [16, D], F32),
            ("bfcT", [16, FF], F32),
            ("apj", [16, FF], F32),
            ("bpjT", [16, D], F32),
            ("bfc2d", [128, NFC], F32),
            ("bpj2d", [128, NKC], F32),
            ("ident", [128, 128], BF16),
            ("tril", [128, 128], BF16),
        ]:
            self.din[name] = nc.dram_tensor(name, shape, dt,
                                            kind="ExternalInput").ap()
        self.outT = nc.dram_tensor("outT", [D, T], F32, kind="ExternalOutput").ap()
        self.scr_fc = nc.dram_tensor("scr_fc", [8, 128, 8, 512], BF16,
                                     kind="Internal").ap()
        self.scr_pj = nc.dram_tensor("scr_pj", [2, 128, 32, 512], BF16,
                                     kind="Internal").ap()
        self.scr_acc = nc.dram_tensor("scr_acc", [2, 128], F32,
                                      kind="Internal").ap()
        self.dbg = {}
        if self.debug:
            for name, shape, dt in [
                ("d_latentT", [128, NLC * T], BF16),
                ("d_qT", [128, NKC * T], BF16),
                ("d_kvT", [128, T], BF16),
                ("d_negc", [128, 1], F32),
                ("d_yT", [128, NKC * T], BF16),
                ("d_x1T", [128, NKC * T], F32),
                ("d_xn2T", [128, NKC * T], BF16),
                ("d_wfc", [8, 128, 8, 512], BF16),
                ("d_wpj", [2, 128, 32, 512], BF16),
                ("d_sum0", [1, 2048], F32),
                ("d_pre0", [1, 2048], F32),
                ("d_post0", [1, 2048], F32),
                ("d_rnbf0", [1, 2048], F32),
                ("d_rn0", [1, 2048], F32),
                ("d_rnB0", [64, 2048], F32),
                ("d_yex0", [64, 2048], F32),
            ]:
                self.dbg[name] = nc.dram_tensor(name, shape, dt,
                                                kind="ExternalOutput").ap()

        self.arena = nc.alloc_sbuf_tensor("arena", [128, 206 * KB],
                                          mybir.dt.uint8)[:, :]
        self.PB = nc.alloc_psum_tensor("pb", [128, 4096], F32)[:, :]

        self.O = dict(
            CONST=0,
            YT=4 * KB,
            XSEG=36 * KB,
            RN=36 * KB,
            XNB=68 * KB,
            LATT=84 * KB,
            ATT=84 * KB,
            QT=100 * KB,
            KVT=132 * KB,
            VEXT=136 * KB,
            WCD=139 * KB,
            WQUP=147 * KB,
            WKV=155 * KB,
            STATS=156 * KB,
            P1=172 * KB,
            MISC=188 * KB,
            X1T=36 * KB,
            XN2=100 * KB,
            WCPROJ=132 * KB,
            XRES=148 * KB,
            STATS2=156 * KB,
            P2=172 * KB,
            HT=132 * KB,
            WFC_D=164 * KB,
            WPJ_D=180 * KB,
        )

    def view(self, off, dt, shape, part0=0, nparts=128):
        esz = {F32: 4, BF16: 2, I32: 4}[dt]
        n = 1
        for s in shape:
            n *= s
        v = self.arena[part0:part0 + nparts, off:off + n * esz].bitcast(dt)
        if len(shape) > 1:
            dims = " ".join(f"d{i}" for i in range(len(shape)))
            kw = {f"d{i}": shape[i] for i in range(1, len(shape))}
            v = v.rearrange(f"p ({dims}) -> p {dims}", **kw)
        return v

    def pbank(self, b, w=512):
        return self.PB[:, b * 512: b * 512 + w]

    def build(self):
        self.alloc()
        O, v = self.O, self.view
        self.c_ident = v(O["CONST"], BF16, [128])
        self.c_tril = v(O["CONST"] + 256, BF16, [128])
        self.c_ones_col = v(O["CONST"] + 512, BF16, [1])
        self.c_ones_row = v(O["CONST"] + 576, BF16, [128], nparts=1)
        self.c_ones_row_f = v(O["CONST"] + 832, F32, [128], nparts=1)
        self.c_bfc = v(O["CONST"] + 1408, F32, [NFC])
        self.c_bpj = v(O["CONST"] + 1600, F32, [NKC])
        self.c_negc = v(O["CONST"] + 1664, F32, [1])
        self.c_rscale_fc = v(O["CONST"] + 1728, F32, [1])
        self.c_scale_fc = v(O["CONST"] + 1792, F32, [1])
        self.c_rscale_pj = v(O["CONST"] + 1856, F32, [1])
        self.c_scale_pj = v(O["CONST"] + 1920, F32, [1])
        self.c_eps = v(O["CONST"] + 1984, F32, [1])
        self.c_shift = v(O["CONST"] + 2048, F32, [1])
        self.c_nsc_fc = v(O["CONST"] + 2112, F32, [1])
        self.c_nb_fc = v(O["CONST"] + 2176, F32, [1])
        self.c_nsc_pj = v(O["CONST"] + 2240, F32, [1])
        self.c_nb_pj = v(O["CONST"] + 2304, F32, [1])
        self.qT = v(O["QT"], BF16, [NKC, T])
        self.kvT = v(O["KVT"], BF16, [T])
        self.vext = v(O["VEXT"], BF16, [NT, 65])
        self.latT = v(O["LATT"], BF16, [NLC, T])
        self.yT = v(O["YT"], BF16, [NKC, T])

        self.phase_A()
        self.phase_B()
        self.phase_C()
        self.phase_D()
        if self.debug:
            self.emit_debug_taps_final()
        self.wait("sync", "dout")

        with ExitStack() as bctx:
            for s in list(self.cnt.keys()):
                self.handles[s] = bctx.enter_context(self.nc.semaphore(s))
            block = bctx.enter_context(self.nc.Block())
            for eng in ENGS:
                def body(e, eng=eng):
                    for fn in self.q[eng]:
                        fn(e)
                getattr(block, eng)(body)
        return self.nc

    def emit_layernorm_seg(self, src, stats_off, dst, pre_cast_wait=None):
        P, v = self, self.view
        m_bf = v(stats_off, BF16, [SEG], nparts=1)
        m_f = v(stats_off + KB, F32, [SEG], nparts=1)
        msq = v(stats_off + 3 * KB, F32, [SEG], nparts=1)
        var = v(stats_off + 5 * KB, F32, [SEG], nparts=1)
        stdr = v(stats_off + 7 * KB, F32, [SEG], nparts=1)
        rstd_bf = v(stats_off + 9 * KB, BF16, [SEG], nparts=1)
        xb = [v(stats_off + 10 * KB + b * KB, BF16, [SEG]) for b in range(2)]
        xsq = [v(stats_off + 12 * KB + b * KB, BF16, [SEG]) for b in range(2)]
        tscr = v(stats_off + 14 * KB, F32, [SEG])
        ps_sum = self.PB[0:1, 0:SEG]
        ps_ssq = self.PB[0:1, 512:512 + SEG]
        psB_m = self.pbank(2)
        psB_r = self.pbank(3)
        if pre_cast_wait is not None:
            pre_cast_wait()
        for kc in range(NKC):
            b = kc % 2
            if kc >= 2:
                P.wait("vector", "pe")
                P.wait("scalar", "pe")
            P.inc("vector", "dve", 1, lambda e, kc=kc, b=b: e.tensor_copy(
                xb[b], src[kc]))
            P.inc("scalar", "act", 1, lambda e, kc=kc, b=b: e.activation(
                xsq[b], src[kc], AF.Square))
            P.wait("tensor", "dve")
            P.wait("tensor", "act")
            P.inc("tensor", "pe", 1, P.mm(ps_sum, self.c_ones_col, xb[b],
                                          start=(kc == 0), stop=(kc == NKC - 1)))
            P.inc("tensor", "pe", 1, P.mm(ps_ssq, self.c_ones_col, xsq[b],
                                          start=(kc == 0), stop=(kc == NKC - 1)))
        P.wait("vector", "pe")
        P.inc("vector", "dve", 1, lambda e: e.tensor_scalar(
            m_f, ps_sum, 1.0 / D, None, ALU.mult))
        P.wait("vector", "dve")
        P.inc("vector", "dve", 1, lambda e: e.tensor_copy(m_bf, m_f))
        P.inc("vector", "dve", 1, lambda e: e.tensor_mul(msq, m_f, m_f))
        P.wait("vector", "dve")
        P.inc("vector", "dve", 1, lambda e: e.scalar_tensor_tensor(
            var, ps_ssq, 1.0 / D, msq, ALU.mult, ALU.subtract))
        P.wait("scalar", "dve")
        P.inc("scalar", "act", 1, lambda e: e.activation(
            stdr, var, AF.Sqrt, bias=self.c_eps[0:1, 0:1]))
        P.wait("vector", "act")
        P.inc("vector", "dve", 1, lambda e: e.reciprocal(stdr, stdr))
        P.wait("vector", "dve")
        P.inc("vector", "dve", 1, lambda e: e.tensor_copy(rstd_bf, stdr))
        P.wait("tensor", "dve")
        P.inc("tensor", "pe", 1, P.mm(psB_m, self.c_ones_row, m_bf))
        P.inc("tensor", "pe", 1, P.mm(psB_r, self.c_ones_row, rstd_bf))
        P.wait("vector", "pe")
        for kc in range(NKC):
            P.inc("vector", "dve", 1, lambda e, kc=kc: e.tensor_sub(
                tscr, src[kc], psB_m))
            P.wait("vector", "dve")
            P.inc("vector", "dve", 1, lambda e, kc=kc: e.tensor_mul(
                dst[kc], tscr, psB_r))

    def phase_A(self):
        P, v, O, din = self, self.view, self.O, self.din
        P.inc("sync", "dx", 16, lambda e: e.dma_start(self.c_ident, din["ident"]))
        P.inc("sync", "dx", 16, lambda e: e.dma_start(self.c_tril, din["tril"]))
        P.inc("sync", "dx", 16, lambda e: e.dma_start(self.c_bfc, din["bfc2d"]))
        P.inc("sync", "dx", 16, lambda e: e.dma_start(self.c_bpj, din["bpj2d"]))
        w_cd = v(O["WCD"], BF16, [NKC, 512])
        w_qup = v(O["WQUP"], BF16, [NLC, 1024])
        w_kv = v(O["WKV"], BF16, [NLC, 128])
        P.inc("gpsimd", "dw", 16, lambda e: e.dma_start(
            w_cd, din["wcd"].rearrange("(kc p) o -> p kc o", p=128)))
        P.inc("gpsimd", "dw", 16, lambda e: e.dma_start(
            w_qup, din["wqup"].rearrange("(kc p) o -> p kc o", p=128)))
        P.inc("gpsimd", "dw", 16, lambda e: e.dma_start(
            w_kv, din["wkv"].rearrange("(kc p) o -> p kc o", p=128)))
        P.inc("vector", "dve", 1, lambda e: e.memset(self.c_ones_col, 1.0))
        P.inc("vector", "dve", 1, lambda e: e.memset(self.c_ones_row, 1.0))
        P.inc("vector", "dve", 1, lambda e: e.memset(self.c_ones_row_f, 1.0))
        P.inc("vector", "dve", 1, lambda e: e.memset(self.c_eps, 1e-5))
        P.inc("vector", "dve", 1, lambda e: e.memset(self.c_shift, 256.5))

        xseg = [v(O["XSEG"] + b * 16 * KB, F32, [NKC, SEG]) for b in range(2)]
        xnb = [v(O["XNB"] + b * 8 * KB, BF16, [NKC, SEG]) for b in range(2)]

        seg_done_dve = {}
        seg_done_act = {}
        seg_done_pe = {}
        for seg in range(NSEG):
            b = seg % 2
            if seg >= 2:
                P.wait("sync", "dve", seg_done_dve[seg - 2])
                P.wait("sync", "act", seg_done_act[seg - 2])
            for kc in range(NKC):
                P.inc("sync", "dx", 16, lambda e, kc=kc, seg=seg, b=b: e.dma_start(
                    xseg[b][:, kc, :], din["xT"][kc * 128:(kc + 1) * 128,
                                           seg * SEG:(seg + 1) * SEG]))

            def pre_cast_wait(seg=seg):
                P.wait("vector", "dx")
                P.wait("scalar", "dx")
                if seg >= 2:
                    P.wait("vector", "pe", seg_done_pe[seg - 2])

            src = [xseg[b][:, kc, :] for kc in range(NKC)]
            dst = [xnb[b][:, kc, :] for kc in range(NKC)]
            self.emit_layernorm_seg(src, O["STATS"], dst, pre_cast_wait)
            seg_done_dve[seg] = P.ck("dve")
            seg_done_act[seg] = P.ck("act")
            P.wait("tensor", "act")
            P.wait("tensor", "dve")
            for kc in range(NKC):
                for oc in range(NLC):
                    P.inc("tensor", "pe", 1, P.mm(
                        self.pbank(4 + oc),
                        v(O["WCD"], BF16, [NKC, 512])[:, kc, oc * 128:(oc + 1) * 128],
                        xnb[b][:, kc, :], start=(kc == 0), stop=(kc == NKC - 1)))
            seg_done_pe[seg] = P.ck("pe")
            P.wait("scalar", "pe")
            for oc in range(NLC):
                P.inc("scalar", "act", 1, lambda e, oc=oc, seg=seg: e.activation(
                    self.latT[:, oc, seg * SEG:(seg + 1) * SEG],
                    self.pbank(4 + oc), AF.Copy))

        P.wait("tensor", "act")
        for lc in range(NLC):
            for seg in range(NSEG):
                P.inc("tensor", "pe", 1, P.mm(
                    self.pbank(4 + seg),
                    v(O["WKV"], BF16, [NLC, 128])[:, lc, :],
                    self.latT[:, lc, seg * SEG:(seg + 1) * SEG],
                    start=(lc == 0), stop=(lc == NLC - 1)))
        P.wait("scalar", "pe")
        for seg in range(NSEG):
            P.inc("scalar", "act", 1, lambda e, seg=seg: e.activation(
                self.kvT[:, seg * SEG:(seg + 1) * SEG],
                self.pbank(4 + seg), AF.Copy))

        for oc in range(NKC):
            P.wait("tensor", "act")
            for lc in range(NLC):
                for seg in range(NSEG):
                    P.inc("tensor", "pe", 1, P.mm(
                        self.pbank(seg),
                        v(O["WQUP"], BF16, [NLC, 1024])[:, lc, oc * 128:(oc + 1) * 128],
                        self.latT[:, lc, seg * SEG:(seg + 1) * SEG],
                        start=(lc == 0), stop=(lc == NLC - 1)))
            P.wait("scalar", "pe")
            for seg in range(NSEG):
                P.inc("scalar", "act", 1, lambda e, oc=oc, seg=seg: e.activation(
                    self.qT[:, oc, seg * SEG:(seg + 1) * SEG],
                    self.pbank(seg), AF.Copy, scale=0.125))

        self.k2 = v(196 * KB, BF16, [T], part0=64, nparts=64)
        P.wait("vector", "act")
        P.inc("vector", "dve", 1, lambda e: e.tensor_copy(
            self.k2, self.kvT[0:64, :]))

        P.inc("vector", "dve", 1, lambda e: e.memset(
            v(O["VEXT"], BF16, [NT * 65]), 1.0))
        ps_tr = self.PB[:, 2048:2048 + 32].bitcast(BF16)
        P.wait("tensor", "act")
        for tt in range(NT):
            P.wait("tensor", "dve")
            P.inc("tensor", "pe", 1, lambda e, tt=tt: e.transpose(
                ps_tr, self.kvT[64:128, tt * 128:(tt + 1) * 128],
                self.c_ident[64:128, 64:128]))
            P.wait("vector", "pe")
            P.inc("vector", "dve", 1, lambda e, tt=tt: e.tensor_copy(
                self.vext[:, tt, 0:64], ps_tr))

        M = O["MISC"]
        qacc = v(M + 2 * KB, BF16, [1])
        tmp1 = v(M + 2 * KB + 64, BF16, [1])
        r128 = v(M + 2 * KB + 128, BF16, [128], nparts=1)
        scb = v(M + 2 * KB + 512, BF16, [8], nparts=1)
        scf = v(M + 3 * KB, F32, [8], nparts=1)
        P.wait("vector", "act")
        for oc in range(NKC):
            if oc == 0:
                P.inc("vector", "dve", 1, lambda e: e.reduce_max(
                    qacc, self.qT[:, 0, :], axis=AX.X, apply_absolute_value=True))
            else:
                P.inc("vector", "dve", 1, lambda e, oc=oc: e.reduce_max(
                    tmp1, self.qT[:, oc, :], axis=AX.X, apply_absolute_value=True))
                P.wait("vector", "dve")
                P.inc("vector", "dve", 1, lambda e: e.tensor_max(qacc, qacc, tmp1))
        ps_r = self.PB[0:1, 2048 + 256:2048 + 256 + 64].bitcast(BF16)
        P.wait("tensor", "dve")
        P.inc("tensor", "pe", 1, lambda e: e.transpose(ps_r, qacc, self.c_ident))
        P.wait("vector", "pe")
        P.inc("vector", "dve", 1, lambda e: e.tensor_copy(r128, ps_r))
        P.wait("vector", "dve")
        P.inc("vector", "dve", 1, lambda e: e.reduce_max(
            scb[:, 0:1], r128, axis=AX.X))
        P.inc("vector", "dve", 1, lambda e: e.reduce_max(
            tmp1[0:64, :], self.kvT[0:64, :], axis=AX.X,
            apply_absolute_value=True))
        P.wait("tensor", "dve")
        P.inc("tensor", "pe", 1, lambda e: e.transpose(
            ps_r[0:1, 0:64], tmp1[0:64, :], self.c_ident[0:64, 0:64]))
        P.wait("vector", "pe")
        P.inc("vector", "dve", 1, lambda e: e.tensor_copy(
            r128[0:1, 0:64], ps_r[0:1, 0:64]))
        P.wait("vector", "dve")
        P.inc("vector", "dve", 1, lambda e: e.reduce_max(
            scb[:, 1:2], r128[0:1, 0:64], axis=AX.X))
        P.wait("vector", "dve")
        P.inc("vector", "dve", 1, lambda e: e.tensor_mul(
            scf[:, 0:1], scb[:, 0:1], scb[:, 1:2]))
        P.wait("vector", "dve")
        P.inc("vector", "dve", 1, lambda e: e.tensor_scalar(
            scf[:, 1:2], scf[:, 0:1], -64.0, None, ALU.mult))
        P.wait("tensor", "dve")
        P.inc("tensor", "pe", 1, P.mm(self.PB[:, 2048 + 512:2048 + 513],
                                      self.c_ones_row_f, scf[:, 1:2]))
        P.wait("vector", "pe")
        P.inc("vector", "dve", 1, lambda e: e.tensor_copy(
            self.c_negc, self.PB[:, 2048 + 512:2048 + 513]))
        self.negc_ready = P.ck("dve")

        self.p1buf = [v(O["P1"] + b * 8 * KB, F32, [2048]) for b in range(2)]
        self.wfc_acc = v(M + 4 * KB, F32, [1])
        self.wpj_acc = v(M + 4 * KB + 64, F32, [1])
        self.p1tmp = v(M + 4 * KB + 128, F32, [1])
        self.p1_tiles = []
        for kc in range(NKC):
            for half in range(2):
                self.p1_tiles.append(
                    (din["wfcT"][kc * 128:(kc + 1) * 128,
                                 half * 2048:(half + 1) * 2048], 0))
        for c4 in range(16):
            self.p1_tiles.append(
                (din["wpjT"][c4 * 256:(c4 + 1) * 256, :]
                 .rearrange("(a p) o -> p a o", p=128), 1))
        self.p1_dma_ck = {}
        self.p1_red_ck = {}
        self.p1_dma(0)
        self.p1_dma(1)

    def p1_dma(self, i):
        P = self
        if i >= len(self.p1_tiles):
            return
        if i >= 2:
            P.wait("sync", "dve", self.p1_red_ck[i - 2])
        src, _ = self.p1_tiles[i]
        dst = self.p1buf[i % 2]
        if len(src.shape) == 3:
            dst = dst.rearrange("p (a o) -> p a o", a=src.shape[1])
        P.inc("sync", "dwf", 16, lambda e, i=i, src=src, dst=dst: e.dma_start(
            dst, src))
        self.p1_dma_ck[i] = P.ck("dwf")

    def p1_reduce(self, i):
        P = self
        if i < 0 or i >= len(self.p1_tiles):
            return
        _, acc = self.p1_tiles[i]
        accv = self.wfc_acc if acc == 0 else self.wpj_acc
        first = (i == 0) or (acc == 1 and self.p1_tiles[i - 1][1] == 0)
        P.wait("vector", "dwf", self.p1_dma_ck[i])
        if first:
            P.inc("vector", "dve", 1, lambda e, i=i, accv=accv: e.reduce_max(
                accv, self.p1buf[i % 2], axis=AX.X, apply_absolute_value=True))
        else:
            P.inc("vector", "dve", 1, lambda e, i=i: e.reduce_max(
                self.p1tmp, self.p1buf[i % 2], axis=AX.X,
                apply_absolute_value=True))
            P.wait("vector", "dve")
            P.inc("vector", "dve", 1, lambda e, accv=accv: e.tensor_max(
                accv, accv, self.p1tmp))
        self.p1_red_ck[i] = P.ck("dve")

    def phase_B(self):
        P, v, O = self, self.view, self.O
        rn_row = v(O["RN"], F32, [T], nparts=1)
        rn_bf = v(O["RN"] + 8 * KB, BF16, [T], nparts=1)
        rnB = v(O["RN"] + 12 * KB, F32, [T], nparts=64)
        attT = [v(O["ATT"] + b * 2 * KB, BF16, [1024]) for b in range(2)]
        sc_ps = [self.PB[:, 0:1024], self.PB[:, 1024:2048]]
        yext = self.PB[:, 2048:4096]

        def q_mov(qt, par, cg):
            base = 64 * par
            return self.qT[base:base + 64, cg * 4:(cg + 1) * 4,
                           qt * 128:(qt + 1) * 128]

        exp_ck = {}
        av_ck = {}
        mask_ck = {}
        yev_ck = 0
        p1_next = 2
        j = 0
        pend = []

        def emit_av(item):
            jj, qt, sc, hg, buf = item
            P.wait("tensor", "act", exp_ck[jj])
            if sc == qt:
                P.wait("tensor", "dve", mask_ck[jj])
            P.inc("tensor", "pe", 1, P.mm(
                yext[0:65, hg * 1024: hg * 1024 + 512],
                self.vext[:, sc, :], attT[buf][:, 0:512],
                start=(sc == 0), stop=(sc == qt)))
            av_ck[jj] = P.inc("tensor", "pe", 1, P.mm(
                yext[0:65, hg * 1024 + 512: hg * 1024 + 1024],
                self.vext[:, sc, :], attT[buf][:, 512:1024],
                start=(sc == 0), stop=(sc == qt)))

        for qt in range(NT):
            for sc in range(qt + 1):
                for hg in range(2):
                    buf = j % 2
                    if j >= 2:
                        P.wait("tensor", "act", exp_ck[j - 2])
                    if sc == 0 and hg == 0 and qt > 0:
                        P.wait("tensor", "dve", yev_ck)
                    lhsK = self.kvT[0:64, sc * 128:(sc + 1) * 128]
                    P.inc("tensor", "pe", 1, P.mm(
                        sc_ps[buf][:, 0:512], lhsK, q_mov(qt, 0, hg),
                        tp=(0, 0)))
                    qk_ck = P.inc("tensor", "pe", 1, P.mm(
                        sc_ps[buf][:, 512:1024],
                        self.k2[:, sc * 128:(sc + 1) * 128],
                        q_mov(qt, 1, hg), tp=(64, 0)))
                    if pend:
                        emit_av(pend.pop(0))
                    P.wait("scalar", "pe", qk_ck)
                    if j >= 2:
                        P.wait("scalar", "pe", av_ck.get(j - 2, 0))
                    if j == 0:
                        P.wait("scalar", "dve", self.negc_ready)
                    P.inc("scalar", "act", 1, lambda e, buf=buf: e.activation(
                        attT[buf], sc_ps[buf], AF.Exp,
                        bias=self.c_negc[:, 0:1]))
                    exp_ck[j] = P.ck("act")
                    if sc == qt:
                        P.wait("vector", "act", exp_ck[j])
                        for jj in range(8):
                            P.inc("vector", "dve", 1,
                                  lambda e, buf=buf, jj=jj: e.tensor_mul(
                                      attT[buf][:, jj * 128:(jj + 1) * 128],
                                      attT[buf][:, jj * 128:(jj + 1) * 128],
                                      self.c_tril))
                        mask_ck[j] = P.ck("dve")
                    pend.append((j, qt, sc, hg, buf))
                    j += 1
            while pend:
                emit_av(pend.pop(0))
            P.wait("vector", "pe", av_ck[j - 1])
            if self.debug and qt == 0:
                s_pre = self.view(64 * KB, F32, [T], nparts=1)
                s_post = self.view(72 * KB, F32, [T], nparts=1)
                P.inc("vector", "dve", 1, lambda e, s_pre=s_pre: e.tensor_copy(
                    s_pre, yext[64:65, :]))
            P.inc("vector", "dve", 1, lambda e: e.reciprocal(
                rn_row, yext[64:65, :]))
            P.wait("vector", "dve")
            if self.debug and qt == 0:
                P.inc("vector", "dve", 1, lambda e, s_post=s_post: e.tensor_copy(
                    s_post, rn_row))
            P.inc("vector", "dve", 1, lambda e: e.tensor_copy(rn_bf, rn_row))
            P.wait("tensor", "dve")
            for q4 in range(4):
                reg = sc_ps[q4 // 2]
                P.inc("tensor", "pe", 1, P.mm(
                    reg[0:64, (q4 % 2) * 512:(q4 % 2) * 512 + 512],
                    self.c_ones_row[0:1, 0:64],
                    rn_bf[0:1, q4 * 512:(q4 + 1) * 512]))
            P.wait("vector", "pe")
            for q4 in range(4):
                reg = sc_ps[q4 // 2]
                P.inc("vector", "dve", 1,
                      lambda e, q4=q4, reg=reg: e.tensor_copy(
                          rnB[:, q4 * 512:(q4 + 1) * 512],
                          reg[0:64, (q4 % 2) * 512:(q4 % 2) * 512 + 512]))
            P.wait("vector", "dve")
            for par in range(2):
                for r in range(2):
                    cols0 = r * 1024 + par * 512
                    dst = self.yT[par * 64:par * 64 + 64, r * 4:(r + 1) * 4,
                                  qt * 128:(qt + 1) * 128]
                    P.inc("vector", "dve", 1,
                          lambda e, dst=dst, cols0=cols0: e.tensor_mul(
                              dst, yext[0:64, cols0:cols0 + 512],
                              rnB[:, cols0:cols0 + 512]))
            if self.debug and qt == 0:
                dbg = self.dbg
                sum0 = v(O["MISC"] + 5 * KB, F32, [T], nparts=1) if False else                     self.view(O["RN"] + 8 * KB, BF16, [T], nparts=1)
                s_tmp = self.view(O["MISC"], F32, [T], nparts=1)
                P.inc("vector", "dve", 1, lambda e, s_tmp=s_tmp: e.tensor_copy(
                    s_tmp, yext[64:65, :]))
                yx_tmp = self.view(O["QT"], BF16, [NKC, T])
                yx = self.view(56 * KB, F32, [T], nparts=64)
                P.inc("vector", "dve", 1, lambda e, yx=yx: e.tensor_copy(
                    yx, yext[0:64, :]))
                dd = P.ck("dve")
                P.wait("sync", "dve", dd)
                P.inc("sync", "dout", 16, lambda e, s_tmp=s_tmp: e.dma_start(
                    dbg["d_sum0"], s_tmp))
                P.inc("sync", "dout", 16, lambda e: e.dma_start(
                    dbg["d_rn0"], rn_row))
                P.inc("sync", "dout", 16, lambda e: e.dma_start(
                    dbg["d_rnB0"], rnB))
                P.inc("sync", "dout", 16, lambda e, yx=yx: e.dma_start(
                    dbg["d_yex0"], yx))
                P.inc("sync", "dout", 16, lambda e: e.dma_start(
                    dbg["d_pre0"], self.view(64 * KB, F32, [T], nparts=1)))
                P.inc("sync", "dout", 16, lambda e: e.dma_start(
                    dbg["d_post0"], self.view(72 * KB, F32, [T], nparts=1)))
                s_rnbf = self.view(88 * KB, F32, [T], nparts=1)
                P.inc("vector", "dve", 1, lambda e, s_rnbf=s_rnbf: e.tensor_copy(
                    s_rnbf, self.view(self.O["RN"] + 8 * KB, BF16, [T], nparts=1)))
                dd2 = P.ck("dve")
                P.wait("sync", "dve", dd2)
                P.inc("sync", "dout", 16, lambda e, s_rnbf=s_rnbf: e.dma_start(
                    dbg["d_rnbf0"], s_rnbf))
                P.wait("vector", "dout")
            yev_ck = P.ck("dve")
            for _ in range(2):
                self.p1_reduce(p1_next - 2)
                self.p1_dma(p1_next)
                p1_next += 1
        while p1_next - 2 < len(self.p1_tiles):
            self.p1_reduce(p1_next - 2)
            self.p1_dma(p1_next)
            p1_next += 1

        M = O["MISC"]
        sfs = v(M + 5 * KB, F32, [16], nparts=1)
        P.wait("sync", "dve")
        P.inc("sync", "dwf", 16, lambda e: e.dma_start(
            self.scr_acc[0, :], self.wfc_acc))
        P.inc("sync", "dwf", 16, lambda e: e.dma_start(
            self.scr_acc[1, :], self.wpj_acc))
        P.wait("sync", "dwf")
        accrow = v(M + 6 * KB, F32, [2, 128], nparts=1)
        P.inc("sync", "dwf", 16, lambda e: e.dma_start(
            accrow, self.scr_acc.rearrange("a p -> (a p)")))
        acc_row_ck = P.ck("dwf")
        for wi, (rsc, ssc) in enumerate([
                (self.c_rscale_fc, self.c_scale_fc),
                (self.c_rscale_pj, self.c_scale_pj)]):
            P.wait("vector", "dwf", acc_row_ck)
            P.inc("vector", "dve", 1, lambda e, wi=wi: e.reduce_max(
                sfs[:, wi * 4:wi * 4 + 1], accrow[:, wi, :], axis=AX.X))
            P.wait("vector", "dve")
            P.inc("vector", "dve", 1, lambda e, wi=wi: e.reciprocal(
                sfs[:, wi * 4 + 1:wi * 4 + 2], sfs[:, wi * 4:wi * 4 + 1]))
            P.wait("vector", "dve")
            P.inc("vector", "dve", 1, lambda e, wi=wi: e.tensor_scalar(
                sfs[:, wi * 4 + 2:wi * 4 + 3], sfs[:, wi * 4 + 1:wi * 4 + 2],
                127.0, None, ALU.mult))
            P.inc("vector", "dve", 1, lambda e, wi=wi: e.tensor_scalar(
                sfs[:, wi * 4 + 3:wi * 4 + 4], sfs[:, wi * 4:wi * 4 + 1],
                1.0 / 127.0, None, ALU.mult))
            for si, dstv in ((2, rsc), (3, ssc)):
                P.wait("tensor", "dve")
                P.inc("tensor", "pe", 1, P.mm(
                    self.PB[:, 128:129], self.c_ones_row_f,
                    sfs[:, wi * 4 + si:wi * 4 + si + 1]))
                P.wait("vector", "pe")
                P.inc("vector", "dve", 1, lambda e, dstv=dstv: e.tensor_copy(
                    dstv, self.PB[:, 128:129]))
            nsc = self.c_nsc_fc if wi == 0 else self.c_nsc_pj
            nb = self.c_nb_fc if wi == 0 else self.c_nb_pj
            P.wait("vector", "dve")
            P.inc("vector", "dve", 1, lambda e, ssc=ssc, nsc=nsc: e.tensor_scalar(
                nsc, ssc, -1.0, None, ALU.mult))
            P.inc("vector", "dve", 1, lambda e, ssc=ssc, nb=nb: e.tensor_scalar(
                nb, ssc, -256.0, None, ALU.mult))
        self.scales_ready = P.ck("dve")

        if self.debug:
            P.wait("sync", "act")
            P.wait("sync", "dve")
            dbg = self.dbg
            P.inc("sync", "dout", 16, lambda e: e.dma_start(
                dbg["d_latentT"], v(O["LATT"], BF16, [NLC * T])))
            P.inc("sync", "dout", 16, lambda e: e.dma_start(
                dbg["d_qT"], v(O["QT"], BF16, [NKC * T])))
            P.inc("sync", "dout", 16, lambda e: e.dma_start(
                dbg["d_kvT"], self.kvT))
            P.inc("sync", "dout", 16, lambda e: e.dma_start(
                dbg["d_negc"], self.c_negc))
            P.inc("sync", "dout", 16, lambda e: e.dma_start(
                dbg["d_yT"], v(O["YT"], BF16, [NKC * T])))

    def phase_C(self):
        P, v, O, din = self, self.view, self.O, self.din
        x1T = v(O["X1T"], F32, [NKC, T])
        xn2T = v(O["XN2"], BF16, [NKC, T])
        w_cproj = v(O["WCPROJ"], BF16, [NKC, 1024])
        xres = [v(O["XRES"] + b * 2 * KB, F32, [SEG]) for b in range(4)]
        self.x1T, self.xn2T = x1T, xn2T

        P.wait("gpsimd", "dve", self.scales_ready)
        P.inc("gpsimd", "dw", 16, lambda e: e.dma_start(
            w_cproj, din["wcproj"].rearrange("(kc p) o -> p kc o", p=128)))
        PP = O["P2"]
        afcb = v(PP + 8 * KB, BF16, [D], nparts=16)
        bfcb = v(PP + 10 * KB, BF16, [FF], nparts=16)
        apjb = v(PP + 18 * KB, BF16, [FF], nparts=16)
        bpjb = v(PP + 26 * KB, BF16, [D], nparts=16)
        P.inc("gpsimd", "dw", 16, lambda e: e.dma_start(afcb, din["afc"]))
        P.inc("gpsimd", "dw", 16, lambda e: e.dma_start(bfcb, din["bfcT"]))
        P.inc("gpsimd", "dw", 16, lambda e: e.dma_start(apjb, din["apj"]))
        P.inc("gpsimd", "dw", 16, lambda e: e.dma_start(bpjb, din["bpjT"]))
        P.wait("vector", "dw")
        P.inc("vector", "dve", 1, lambda e: e.tensor_scalar(
            afcb, afcb, 2.0, None, ALU.mult))
        P.inc("vector", "dve", 1, lambda e: e.tensor_scalar(
            apjb, apjb, 2.0, None, ALU.mult))
        ab_ready = P.ck("dve")

        xres_ck = {}
        for oc in range(NKC):
            for seg in range(NSEG):
                i = oc * NSEG + seg
                if i >= 4:
                    P.wait("sync", "dve", xres_ck[i - 4])
                P.inc("sync", "dres", 16, lambda e, oc=oc, seg=seg, i=i:
                      e.dma_start(xres[i % 4],
                                  din["xT"][oc * 128:(oc + 1) * 128,
                                            seg * SEG:(seg + 1) * SEG]))
            if oc == 0:
                P.wait("tensor", "dw")
            else:
                P.wait("tensor", "dve")
            for kc in range(NKC):
                for seg in range(NSEG):
                    P.inc("tensor", "pe", 1, P.mm(
                        self.pbank(seg),
                        w_cproj[:, kc, oc * 128:(oc + 1) * 128],
                        self.yT[:, kc, seg * SEG:(seg + 1) * SEG],
                        start=(kc == 0), stop=(kc == NKC - 1)))
            P.wait("vector", "pe")
            P.wait("vector", "dres")
            for seg in range(NSEG):
                i = oc * NSEG + seg
                P.inc("vector", "dve", 1,
                      lambda e, oc=oc, seg=seg, i=i: e.tensor_add(
                          x1T[:, oc, seg * SEG:(seg + 1) * SEG],
                          self.pbank(seg), xres[i % 4]))
                xres_ck[i] = P.ck("dve")

        x1_done = P.ck("dve")

        def ln2_wait():
            P.wait("scalar", "dve", x1_done)

        for seg in range(NSEG):
            src = [x1T[:, kc, seg * SEG:(seg + 1) * SEG] for kc in range(NKC)]
            dst = [xn2T[:, kc, seg * SEG:(seg + 1) * SEG] for kc in range(NKC)]
            self.emit_layernorm_seg(src, O["STATS2"], dst,
                                    ln2_wait if seg == 0 else None)
        self.ln2_done_dve = P.ck("dve")

        p2w = [v(PP + b * 2 * KB, F32, [SEG]) for b in range(2)]
        p2i = v(PP + 4 * KB, I32, [SEG])
        p2u = v(PP + 4 * KB, F32, [SEG])
        p2f = v(PP + 6 * KB, F32, [SEG])
        p2r = v(PP + 30 * KB, F32, [SEG])
        p2s = [v(PP + 28 * KB + b * KB, BF16, [SEG]) for b in range(2)]
        tiles = []
        for kc in range(NKC):
            for cb in range(8):
                tiles.append(("fc", kc, cb))
        for kc2 in range(NFC):
            for cb2 in range(2):
                tiles.append(("pj", kc2, cb2))
        p2_dma_ck = {}
        p2_done = {}
        for i, t in enumerate(tiles):
            kind, kc, cb = t
            src = (din["wfcT"] if kind == "fc" else din["wpjT"])[
                kc * 128:(kc + 1) * 128, cb * 512:(cb + 1) * 512]
            dstd = (self.scr_fc if kind == "fc" else self.scr_pj)[cb, :, kc, :]
            if i >= 2:
                P.wait("sync", "dve", p2_done[i - 2])
            P.inc("sync", "dwf", 16, lambda e, i=i, src=src: e.dma_start(
                p2w[i % 2], src))
            p2_dma_ck[i] = P.ck("dwf")
            ba = self.PB[:, (4 + i % 2) * 512:(4 + i % 2) * 512 + 512]
            if i == 0:
                P.wait("tensor", "dve", ab_ready)
            if i >= 2:
                P.wait("tensor", "dve", p2_done[i - 2])
            if kind == "fc":
                lhs, rhs = afcb[:, kc * 128:(kc + 1) * 128], \
                    bfcb[:, cb * 512:(cb + 1) * 512]
            else:
                lhs, rhs = apjb[:, kc * 128:(kc + 1) * 128], \
                    bpjb[:, cb * 512:(cb + 1) * 512]
            ba_ck = P.inc("tensor", "pe", 1, P.mm(ba, lhs, rhs))
            rsc = self.c_rscale_fc if kind == "fc" else self.c_rscale_pj
            nsc = self.c_nsc_fc if kind == "fc" else self.c_nsc_pj
            nb = self.c_nb_fc if kind == "fc" else self.c_nb_pj
            P.wait("vector", "dwf", p2_dma_ck[i])
            if i == 0:
                P.wait("vector", "dve", self.scales_ready)
            P.inc("vector", "dve", 1, lambda e, i=i, rsc=rsc: e.tensor_scalar(
                p2r, p2w[i % 2], rsc[:, 0:1], 256.5, ALU.mult, ALU.add))
            P.wait("vector", "dve")
            P.inc("vector", "dve", 1, lambda e: e.tensor_copy(p2i, p2r))
            P.wait("vector", "dve")
            P.inc("vector", "dve", 1, lambda e: e.tensor_copy(p2f, p2i))
            P.wait("vector", "dve")
            P.inc("vector", "dve", 1, lambda e: e.tensor_sub(p2r, p2r, p2f))
            P.wait("vector", "dve")
            P.inc("vector", "dve", 1, lambda e: e.scalar_tensor_tensor(
                p2u, p2r, 0.0, p2f, ALU.is_lt, ALU.subtract))
            P.wait("vector", "dve")
            P.inc("vector", "dve", 1, lambda e, nsc=nsc: e.tensor_scalar(
                p2r, p2u, 256.0, nsc[:, 0:1], ALU.add, ALU.mult))
            P.wait("vector", "pe", ba_ck)
            P.wait("vector", "dve")
            if i >= 2:
                P.wait("vector", "dscr", 16 * (i - 1))
            P.inc("vector", "dve", 1, lambda e, i=i, ba=ba: e.tensor_add(
                p2s[i % 2], p2r, ba))
            p2_done[i] = P.ck("dve")
            P.wait("gpsimd", "dve", p2_done[i])
            P.inc("gpsimd", "dscr", 16, lambda e, i=i, dstd=dstd: e.dma_start(
                dstd, p2s[i % 2]))

        if self.debug:
            P.wait("sync", "dve")
            dbg = self.dbg
            P.inc("sync", "dout", 16, lambda e: e.dma_start(
                dbg["d_x1T"], v(O["X1T"], F32, [NKC * T])))
            P.inc("sync", "dout", 16, lambda e: e.dma_start(
                dbg["d_xn2T"], v(O["XN2"], BF16, [NKC * T])))
            P.wait("sync", "dscr")
            for cb in range(8):
                P.inc("sync", "dout", 16, lambda e, cb=cb: e.dma_start(
                    dbg["d_wfc"][cb], self.scr_fc[cb]))
            for cb in range(2):
                P.inc("sync", "dout", 16, lambda e, cb=cb: e.dma_start(
                    dbg["d_wpj"][cb], self.scr_pj[cb]))

    def phase_D(self):
        P, v, O = self, self.view, self.O
        x1T, xn2T = self.x1T, self.xn2T
        hT = v(O["HT"], BF16, [NFC, SEG])
        wfc_t = [v(O["WFC_D"] + b * 8 * KB, BF16, [NKC, 512]) for b in range(2)]
        wpj_t = v(O["WPJ_D"], BF16, [16, 512])

        scr_written = P.ck("dscr")
        gelu_ck = {}
        fc_free_ck = {}
        prev_evac_dve = 0
        for gseg in range(NSEG):
            gcols = slice(gseg * SEG, (gseg + 1) * SEG)
            for ocg in range(8):
                i = gseg * 8 + ocg
                if i == 0:
                    P.wait("sync", "dscr", scr_written)
                if i >= 2:
                    P.wait("sync", "pe", fc_free_ck[i - 2])
                P.inc("sync", "dffn", 16, lambda e, ocg=ocg, i=i: e.dma_start(
                    wfc_t[i % 2], self.scr_fc[ocg]))
                dck = P.ck("dffn")
                P.wait("tensor", "dffn", dck)
                if i == 0:
                    P.wait("tensor", "dve", self.ln2_done_dve)
                for jj in range(4):
                    oc = ocg * 4 + jj
                    gi = i * 4 + jj
                    pb = self.pbank(jj % 2)
                    if gi >= 2:
                        P.wait("tensor", "act", gelu_ck[gi - 2])
                    for kc in range(NKC):
                        P.inc("tensor", "pe", 1, P.mm(
                            pb, wfc_t[i % 2][:, kc, jj * 128:(jj + 1) * 128],
                            xn2T[:, kc, gcols],
                            start=(kc == 0), stop=(kc == NKC - 1)))
                    mmck = P.ck("pe")
                    P.wait("scalar", "pe", mmck)
                    P.inc("scalar", "act", 1,
                          lambda e, oc=oc, pb=pb: e.activation(
                              hT[:, oc, :], pb, AF.Gelu,
                              bias=self.c_bfc[:, oc:oc + 1]))
                    gelu_ck[gi] = P.ck("act")
                fc_free_ck[i] = P.ck("pe")
            P.wait("tensor", "act", P.ck("act"))
            for cb2 in range(2):
                if prev_evac_dve:
                    P.wait("tensor", "dve", prev_evac_dve)
                for kh in range(2):
                    P.wait("sync", "pe")
                    P.inc("sync", "dffn", 16,
                          lambda e, cb2=cb2, kh=kh: e.dma_start(
                              wpj_t, self.scr_pj[cb2, :, kh * 16:(kh + 1) * 16, :]))
                    P.wait("tensor", "dffn")
                    for jj in range(4):
                        pb = self.PB[:, (4 + jj) * 512:(4 + jj) * 512 + 512]
                        for k16 in range(16):
                            kc2 = kh * 16 + k16
                            P.inc("tensor", "pe", 1, P.mm(
                                pb, wpj_t[:, k16, jj * 128:(jj + 1) * 128],
                                hT[:, kc2, :],
                                start=(kh == 0 and k16 == 0),
                                stop=(kh == 1 and k16 == 15)))
                P.wait("vector", "pe")
                for jj in range(4):
                    oc2 = cb2 * 4 + jj
                    pb = self.PB[:, (4 + jj) * 512:(4 + jj) * 512 + 512]
                    P.inc("vector", "dve", 1,
                          lambda e, oc2=oc2, pb=pb, gcols=gcols:
                          e.scalar_tensor_tensor(
                              x1T[:, oc2, gcols], pb,
                              self.c_bpj[:, oc2:oc2 + 1],
                              x1T[:, oc2, gcols], ALU.add, ALU.add))
                    dveck = P.ck("dve")
                    P.wait("sync", "dve", dveck)
                    P.inc("sync", "dout", 16,
                          lambda e, oc2=oc2, gcols=gcols: e.dma_start(
                              self.outT[oc2 * 128:(oc2 + 1) * 128, gcols],
                              x1T[:, oc2, gcols]))
                prev_evac_dve = P.ck("dve")

    def emit_debug_taps_final(self):
        pass


def build_program(debug=False):
    b = Builder(debug=debug)
    nc = b.build()
    return b, nc


def kernel(L0_attn_x, L0_attn_Wc_down, L0_attn_Wq_up, L0_attn_Wkv_up,
           L0_attn_Wc_proj, L0_ffn_W_fc, L0_ffn_b_fc, L0_ffn_A_fc,
           L0_ffn_B_fc, L0_ffn_W_proj, L0_ffn_b_proj, L0_ffn_A_proj,
           L0_ffn_B_proj):
    b, nc = build_program(debug=False)

    bf = ml_dtypes.bfloat16
    shared = {
        "wcd": np.ascontiguousarray(L0_attn_Wc_down, dtype=np.float32),
        "wqup": np.ascontiguousarray(L0_attn_Wq_up, dtype=np.float32),
        "wkv": np.ascontiguousarray(L0_attn_Wkv_up, dtype=np.float32),
        "wcproj": np.ascontiguousarray(L0_attn_Wc_proj, dtype=np.float32),
        "wfcT": np.ascontiguousarray(np.asarray(L0_ffn_W_fc).T, dtype=np.float32),
        "wpjT": np.ascontiguousarray(np.asarray(L0_ffn_W_proj).T, dtype=np.float32),
        "afc": np.ascontiguousarray(L0_ffn_A_fc, dtype=np.float32),
        "bfcT": np.ascontiguousarray(np.asarray(L0_ffn_B_fc).T, dtype=np.float32),
        "apj": np.ascontiguousarray(L0_ffn_A_proj, dtype=np.float32),
        "bpjT": np.ascontiguousarray(np.asarray(L0_ffn_B_proj).T, dtype=np.float32),
        "bfc2d": np.ascontiguousarray(
            np.asarray(L0_ffn_b_fc).reshape(NFC, 128).T, dtype=np.float32),
        "bpj2d": np.ascontiguousarray(
            np.asarray(L0_ffn_b_proj).reshape(NKC, 128).T, dtype=np.float32),
        "ident": np.eye(128, dtype=bf),
        "tril": (np.tril(np.ones((128, 128), dtype=np.float32)).T
                 ).astype(bf),
    }
    x = np.asarray(L0_attn_x)
    in_maps = []
    for c in range(8):
        m = dict(shared)
        m["xT"] = np.ascontiguousarray(x[c].T, dtype=np.float32)
        in_maps.append(m)
    res = run_bass_kernel_spmd(nc, in_maps, core_ids=list(range(8)))
    outs = []
    for c in range(8):
        o = np.asarray(res.results[c]["outT"])
        outs.append(np.ascontiguousarray(o.T))
    return np.stack(outs, axis=0)


# version 05
# speedup vs baseline: 1.0645x; 1.0645x over previous
import numpy as np
import ml_dtypes
from contextlib import ExitStack

import concourse.bass as bass
import concourse.mybir as mybir
from concourse.bass_utils import run_bass_kernel_spmd

F32 = mybir.dt.float32
BF16 = mybir.dt.bfloat16
I32 = mybir.dt.int32
AF = mybir.ActivationFunctionType
ALU = mybir.AluOpType
AX = mybir.AxisListType

T = 2048
D = 1024
NKC = D // 128
LAT = 512
NLC = LAT // 128
H = 16
HD = 64
FF = 4096
NFC = FF // 128
SEG = 512
NSEG = T // SEG
NT = T // 128
KB = 1024

ENGS = ("sync", "tensor", "vector", "scalar", "gpsimd")


class Builder:
    def __init__(self, debug=False):
        self.debug = debug
        self.nc = bass.Bass("TRN2", target_bir_lowering=False,
                            detect_race_conditions=False)
        self.q = {e: [] for e in ENGS}
        self.cnt = {}
        self.handles = {}
        for s in ("dx", "dw", "dwf", "dscr", "dres", "dffn", "dout",
                  "pe", "act", "dve", "pool"):
            self.cnt[s] = 0

    def emit(self, eng, fn):
        self.q[eng].append(fn)

    def inc(self, eng, sem, n, build):
        self.cnt[sem] += n

        def f(e, build=build, sem=sem, n=n):
            build(e).then_inc(self.handles[sem], n)

        self.q[eng].append(f)
        return self.cnt[sem]

    def wait(self, eng, sem, val=None):
        v = self.cnt[sem] if val is None else val
        if v <= 0:
            return

        def f(e, sem=sem, v=v):
            e.wait_ge(self.handles[sem], v)

        self.q[eng].append(f)

    def ck(self, sem):
        return self.cnt[sem]

    def mm(self, out, lhsT, rhs, start=True, stop=True, tp=None):
        def f(e, out=out, lhsT=lhsT, rhs=rhs, start=start, stop=stop, tp=tp):
            return e.matmul(out, lhsT, rhs, start=start, stop=stop,
                            tile_position=tp, skip_group_check=True)
        return f

    def alloc(self):
        nc = self.nc
        self.din = {}
        for name, shape, dt in [
            ("xT", [D, T], F32),
            ("wcd", [D, LAT], F32),
            ("wqup", [LAT, D], F32),
            ("wkv", [LAT, 2 * HD], F32),
            ("wcproj", [D, D], F32),
            ("wfcT", [D, FF], F32),
            ("wpjT", [FF, D], F32),
            ("afc", [16, D], F32),
            ("bfcT", [16, FF], F32),
            ("apj", [16, FF], F32),
            ("bpjT", [16, D], F32),
            ("bfc2d", [128, NFC], F32),
            ("bpj2d", [128, NKC], F32),
            ("ident", [128, 128], BF16),
            ("tril", [128, 128], BF16),
        ]:
            self.din[name] = nc.dram_tensor(name, shape, dt,
                                            kind="ExternalInput").ap()
        self.outT = nc.dram_tensor("outT", [D, T], F32, kind="ExternalOutput").ap()
        self.scr_fc = nc.dram_tensor("scr_fc", [8, 128, 8, 512], BF16,
                                     kind="Internal").ap()
        self.scr_pj = nc.dram_tensor("scr_pj", [2, 128, 32, 512], BF16,
                                     kind="Internal").ap()
        self.scr_acc = nc.dram_tensor("scr_acc", [2, 128], F32,
                                      kind="Internal").ap()
        self.dbg = {}
        if self.debug:
            for name, shape, dt in [
                ("d_latentT", [128, NLC * T], BF16),
                ("d_qT", [128, NKC * T], BF16),
                ("d_kvT", [128, T], BF16),
                ("d_negc", [128, 1], F32),
                ("d_yT", [128, NKC * T], BF16),
                ("d_x1T", [128, NKC * T], F32),
                ("d_xn2T", [128, NKC * T], BF16),
                ("d_wfc", [8, 128, 8, 512], BF16),
                ("d_wpj", [2, 128, 32, 512], BF16),
                ("d_sum0", [1, 2048], F32),
                ("d_pre0", [1, 2048], F32),
                ("d_post0", [1, 2048], F32),
                ("d_rnbf0", [1, 2048], F32),
                ("d_rn0", [1, 2048], F32),
                ("d_rnB0", [64, 2048], F32),
                ("d_yex0", [64, 2048], F32),
            ]:
                self.dbg[name] = nc.dram_tensor(name, shape, dt,
                                                kind="ExternalOutput").ap()

        self.arena = nc.alloc_sbuf_tensor("arena", [128, 206 * KB],
                                          mybir.dt.uint8)[:, :]
        self.PB = nc.alloc_psum_tensor("pb", [128, 4096], F32)[:, :]

        self.O = dict(
            CONST=0,
            YT=4 * KB,
            XSEG=36 * KB,
            RN=36 * KB,
            XNB=68 * KB,
            LATT=84 * KB,
            ATT=84 * KB,
            QT=100 * KB,
            KVT=132 * KB,
            VEXT=136 * KB,
            WCD=139 * KB,
            WQUP=147 * KB,
            WKV=155 * KB,
            STATS=156 * KB,
            P1=172 * KB,
            MISC=188 * KB,
            X1T=36 * KB,
            XN2=100 * KB,
            WCPROJ=132 * KB,
            XRES=148 * KB,
            STATS2=156 * KB,
            P2=172 * KB,
            HT=132 * KB,
            WFC_D=164 * KB,
            WPJ_D=180 * KB,
        )

    def view(self, off, dt, shape, part0=0, nparts=128):
        esz = {F32: 4, BF16: 2, I32: 4}[dt]
        n = 1
        for s in shape:
            n *= s
        v = self.arena[part0:part0 + nparts, off:off + n * esz].bitcast(dt)
        if len(shape) > 1:
            dims = " ".join(f"d{i}" for i in range(len(shape)))
            kw = {f"d{i}": shape[i] for i in range(1, len(shape))}
            v = v.rearrange(f"p ({dims}) -> p {dims}", **kw)
        return v

    def pbank(self, b, w=512):
        return self.PB[:, b * 512: b * 512 + w]

    def build(self):
        self.alloc()
        O, v = self.O, self.view
        self.c_ident = v(O["CONST"], BF16, [128])
        self.c_tril = v(O["CONST"] + 256, BF16, [128])
        self.c_ones_col = v(O["CONST"] + 512, BF16, [1])
        self.c_ones_row = v(O["CONST"] + 576, BF16, [128], nparts=1)
        self.c_ones_row_f = v(O["CONST"] + 832, F32, [128], nparts=1)
        self.c_bfc = v(O["CONST"] + 1408, F32, [NFC])
        self.c_bpj = v(O["CONST"] + 1600, F32, [NKC])
        self.c_negc = v(O["CONST"] + 1664, F32, [1])
        self.c_rscale_fc = v(O["CONST"] + 1728, F32, [1])
        self.c_scale_fc = v(O["CONST"] + 1792, F32, [1])
        self.c_rscale_pj = v(O["CONST"] + 1856, F32, [1])
        self.c_scale_pj = v(O["CONST"] + 1920, F32, [1])
        self.c_eps = v(O["CONST"] + 1984, F32, [1])
        self.c_shift = v(O["CONST"] + 2048, F32, [1])
        self.c_nsc_fc = v(O["CONST"] + 2112, F32, [1])
        self.c_nb_fc = v(O["CONST"] + 2176, F32, [1])
        self.c_nsc_pj = v(O["CONST"] + 2240, F32, [1])
        self.c_nb_pj = v(O["CONST"] + 2304, F32, [1])
        self.qT = v(O["QT"], BF16, [NKC, T])
        self.kvT = v(O["KVT"], BF16, [T])
        self.vext = v(O["VEXT"], BF16, [NT, 65])
        self.latT = v(O["LATT"], BF16, [NLC, T])
        self.yT = v(O["YT"], BF16, [NKC, T])

        self.phase_A()
        self.phase_B()
        self.phase_C()
        self.phase_D()
        if self.debug:
            self.emit_debug_taps_final()
        self.wait("sync", "dout")

        with ExitStack() as bctx:
            for s in list(self.cnt.keys()):
                self.handles[s] = bctx.enter_context(self.nc.semaphore(s))
            block = bctx.enter_context(self.nc.Block())
            for eng in ENGS:
                def body(e, eng=eng):
                    for fn in self.q[eng]:
                        fn(e)
                getattr(block, eng)(body)
        return self.nc

    def emit_layernorm_seg(self, src, stats_off, dst, pre_cast_wait=None):
        P, v = self, self.view
        m_bf = v(stats_off, BF16, [SEG], nparts=1)
        m_f = v(stats_off + KB, F32, [SEG], nparts=1)
        msq = v(stats_off + 3 * KB, F32, [SEG], nparts=1)
        var = v(stats_off + 5 * KB, F32, [SEG], nparts=1)
        stdr = v(stats_off + 7 * KB, F32, [SEG], nparts=1)
        rstd_bf = v(stats_off + 9 * KB, BF16, [SEG], nparts=1)
        xb = [v(stats_off + 10 * KB + b * KB, BF16, [SEG]) for b in range(2)]
        xsq = [v(stats_off + 12 * KB + b * KB, BF16, [SEG]) for b in range(2)]
        tscr = v(stats_off + 14 * KB, F32, [SEG])
        ps_sum = self.PB[0:1, 0:SEG]
        ps_ssq = self.PB[0:1, 512:512 + SEG]
        psB_m = self.pbank(2)
        psB_r = self.pbank(3)
        if pre_cast_wait is not None:
            pre_cast_wait()
        for kc in range(NKC):
            b = kc % 2
            if kc >= 2:
                P.wait("vector", "pe")
                P.wait("scalar", "pe")
            P.inc("vector", "dve", 1, lambda e, kc=kc, b=b: e.tensor_copy(
                xb[b], src[kc]))
            P.inc("scalar", "act", 1, lambda e, kc=kc, b=b: e.activation(
                xsq[b], src[kc], AF.Square))
            P.wait("tensor", "dve")
            P.wait("tensor", "act")
            P.inc("tensor", "pe", 1, P.mm(ps_sum, self.c_ones_col, xb[b],
                                          start=(kc == 0), stop=(kc == NKC - 1)))
            P.inc("tensor", "pe", 1, P.mm(ps_ssq, self.c_ones_col, xsq[b],
                                          start=(kc == 0), stop=(kc == NKC - 1)))
        P.wait("vector", "pe")
        P.inc("vector", "dve", 1, lambda e: e.tensor_scalar(
            m_f, ps_sum, 1.0 / D, None, ALU.mult))
        P.wait("vector", "dve")
        P.inc("vector", "dve", 1, lambda e: e.tensor_copy(m_bf, m_f))
        P.inc("vector", "dve", 1, lambda e: e.tensor_mul(msq, m_f, m_f))
        P.wait("vector", "dve")
        P.inc("vector", "dve", 1, lambda e: e.scalar_tensor_tensor(
            var, ps_ssq, 1.0 / D, msq, ALU.mult, ALU.subtract))
        P.wait("scalar", "dve")
        P.inc("scalar", "act", 1, lambda e: e.activation(
            stdr, var, AF.Ln, bias=self.c_eps[0:1, 0:1]))
        P.wait("scalar", "act")
        P.inc("scalar", "act", 1, lambda e: e.activation(
            rstd_bf, stdr, AF.Exp, scale=-0.5))
        P.wait("tensor", "dve")
        P.wait("tensor", "act")
        P.inc("tensor", "pe", 1, P.mm(psB_m, self.c_ones_row, m_bf))
        P.inc("tensor", "pe", 1, P.mm(psB_r, self.c_ones_row, rstd_bf))
        P.wait("vector", "pe")
        for kc in range(NKC):
            P.inc("vector", "dve", 1, lambda e, kc=kc: e.tensor_sub(
                tscr, src[kc], psB_m))
            P.wait("vector", "dve")
            P.inc("vector", "dve", 1, lambda e, kc=kc: e.tensor_mul(
                dst[kc], tscr, psB_r))

    def phase_A(self):
        P, v, O, din = self, self.view, self.O, self.din
        P.inc("sync", "dx", 16, lambda e: e.dma_start(self.c_ident, din["ident"]))
        P.inc("sync", "dx", 16, lambda e: e.dma_start(self.c_tril, din["tril"]))
        P.inc("sync", "dx", 16, lambda e: e.dma_start(self.c_bfc, din["bfc2d"]))
        P.inc("sync", "dx", 16, lambda e: e.dma_start(self.c_bpj, din["bpj2d"]))
        w_cd = v(O["WCD"], BF16, [NKC, 512])
        w_qup = v(O["WQUP"], BF16, [NLC, 1024])
        w_kv = v(O["WKV"], BF16, [NLC, 128])
        P.inc("gpsimd", "dw", 16, lambda e: e.dma_start(
            w_cd, din["wcd"].rearrange("(kc p) o -> p kc o", p=128)))
        P.inc("gpsimd", "dw", 16, lambda e: e.dma_start(
            w_qup, din["wqup"].rearrange("(kc p) o -> p kc o", p=128)))
        P.inc("gpsimd", "dw", 16, lambda e: e.dma_start(
            w_kv, din["wkv"].rearrange("(kc p) o -> p kc o", p=128)))
        P.inc("vector", "dve", 1, lambda e: e.memset(self.c_ones_col, 1.0))
        P.inc("vector", "dve", 1, lambda e: e.memset(self.c_ones_row, 1.0))
        P.inc("vector", "dve", 1, lambda e: e.memset(self.c_ones_row_f, 1.0))
        P.inc("vector", "dve", 1, lambda e: e.memset(self.c_eps, 1e-5))
        P.inc("vector", "dve", 1, lambda e: e.memset(self.c_shift, 256.5))

        xseg = [v(O["XSEG"] + b * 16 * KB, F32, [NKC, SEG]) for b in range(2)]
        xnb = [v(O["XNB"] + b * 8 * KB, BF16, [NKC, SEG]) for b in range(2)]

        seg_done_dve = {}
        seg_done_act = {}
        seg_done_pe = {}
        for seg in range(NSEG):
            b = seg % 2
            if seg >= 2:
                P.wait("sync", "dve", seg_done_dve[seg - 2])
                P.wait("sync", "act", seg_done_act[seg - 2])
            for kc in range(NKC):
                P.inc("sync", "dx", 16, lambda e, kc=kc, seg=seg, b=b: e.dma_start(
                    xseg[b][:, kc, :], din["xT"][kc * 128:(kc + 1) * 128,
                                           seg * SEG:(seg + 1) * SEG]))

            def pre_cast_wait(seg=seg):
                P.wait("vector", "dx")
                P.wait("scalar", "dx")
                if seg >= 2:
                    P.wait("vector", "pe", seg_done_pe[seg - 2])

            src = [xseg[b][:, kc, :] for kc in range(NKC)]
            dst = [xnb[b][:, kc, :] for kc in range(NKC)]
            self.emit_layernorm_seg(src, O["STATS"], dst, pre_cast_wait)
            seg_done_dve[seg] = P.ck("dve")
            seg_done_act[seg] = P.ck("act")
            P.wait("tensor", "act")
            P.wait("tensor", "dve")
            for kc in range(NKC):
                for oc in range(NLC):
                    P.inc("tensor", "pe", 1, P.mm(
                        self.pbank(4 + oc),
                        v(O["WCD"], BF16, [NKC, 512])[:, kc, oc * 128:(oc + 1) * 128],
                        xnb[b][:, kc, :], start=(kc == 0), stop=(kc == NKC - 1)))
            seg_done_pe[seg] = P.ck("pe")
            P.wait("scalar", "pe")
            for oc in range(NLC):
                P.inc("scalar", "act", 1, lambda e, oc=oc, seg=seg: e.activation(
                    self.latT[:, oc, seg * SEG:(seg + 1) * SEG],
                    self.pbank(4 + oc), AF.Copy))

        P.wait("tensor", "act")
        for lc in range(NLC):
            for seg in range(NSEG):
                P.inc("tensor", "pe", 1, P.mm(
                    self.pbank(4 + seg),
                    v(O["WKV"], BF16, [NLC, 128])[:, lc, :],
                    self.latT[:, lc, seg * SEG:(seg + 1) * SEG],
                    start=(lc == 0), stop=(lc == NLC - 1)))
        P.wait("scalar", "pe")
        for seg in range(NSEG):
            P.inc("scalar", "act", 1, lambda e, seg=seg: e.activation(
                self.kvT[:, seg * SEG:(seg + 1) * SEG],
                self.pbank(4 + seg), AF.Copy))

        for oc in range(NKC):
            P.wait("tensor", "act")
            for lc in range(NLC):
                for seg in range(NSEG):
                    P.inc("tensor", "pe", 1, P.mm(
                        self.pbank(seg),
                        v(O["WQUP"], BF16, [NLC, 1024])[:, lc, oc * 128:(oc + 1) * 128],
                        self.latT[:, lc, seg * SEG:(seg + 1) * SEG],
                        start=(lc == 0), stop=(lc == NLC - 1)))
            P.wait("scalar", "pe")
            for seg in range(NSEG):
                P.inc("scalar", "act", 1, lambda e, oc=oc, seg=seg: e.activation(
                    self.qT[:, oc, seg * SEG:(seg + 1) * SEG],
                    self.pbank(seg), AF.Copy, scale=0.125))

        self.k2 = v(196 * KB, BF16, [T], part0=64, nparts=64)
        P.wait("vector", "act")
        P.inc("vector", "dve", 1, lambda e: e.tensor_copy(
            self.k2, self.kvT[0:64, :]))

        P.inc("vector", "dve", 1, lambda e: e.memset(
            v(O["VEXT"], BF16, [NT * 65]), 1.0))
        ps_tr = self.PB[:, 2048:2048 + 32].bitcast(BF16)
        P.wait("tensor", "act")
        for tt in range(NT):
            P.wait("tensor", "dve")
            P.inc("tensor", "pe", 1, lambda e, tt=tt: e.transpose(
                ps_tr, self.kvT[64:128, tt * 128:(tt + 1) * 128],
                self.c_ident[64:128, 64:128]))
            P.wait("vector", "pe")
            P.inc("vector", "dve", 1, lambda e, tt=tt: e.tensor_copy(
                self.vext[:, tt, 0:64], ps_tr))

        M = O["MISC"]
        qacc = v(M + 2 * KB, BF16, [1])
        tmp1 = v(M + 2 * KB + 64, BF16, [1])
        r128 = v(M + 2 * KB + 128, BF16, [128], nparts=1)
        scb = v(M + 2 * KB + 512, BF16, [8], nparts=1)
        scf = v(M + 3 * KB, F32, [8], nparts=1)
        P.wait("vector", "act")
        for oc in range(NKC):
            if oc == 0:
                P.inc("vector", "dve", 1, lambda e: e.reduce_max(
                    qacc, self.qT[:, 0, :], axis=AX.X, apply_absolute_value=True))
            else:
                P.inc("vector", "dve", 1, lambda e, oc=oc: e.reduce_max(
                    tmp1, self.qT[:, oc, :], axis=AX.X, apply_absolute_value=True))
                P.wait("vector", "dve")
                P.inc("vector", "dve", 1, lambda e: e.tensor_max(qacc, qacc, tmp1))
        ps_r = self.PB[0:1, 2048 + 256:2048 + 256 + 64].bitcast(BF16)
        P.wait("tensor", "dve")
        P.inc("tensor", "pe", 1, lambda e: e.transpose(ps_r, qacc, self.c_ident))
        P.wait("vector", "pe")
        P.inc("vector", "dve", 1, lambda e: e.tensor_copy(r128, ps_r))
        P.wait("vector", "dve")
        P.inc("vector", "dve", 1, lambda e: e.reduce_max(
            scb[:, 0:1], r128, axis=AX.X))
        P.inc("vector", "dve", 1, lambda e: e.reduce_max(
            tmp1[0:64, :], self.kvT[0:64, :], axis=AX.X,
            apply_absolute_value=True))
        P.wait("tensor", "dve")
        P.inc("tensor", "pe", 1, lambda e: e.transpose(
            ps_r[0:1, 0:64], tmp1[0:64, :], self.c_ident[0:64, 0:64]))
        P.wait("vector", "pe")
        P.inc("vector", "dve", 1, lambda e: e.tensor_copy(
            r128[0:1, 0:64], ps_r[0:1, 0:64]))
        P.wait("vector", "dve")
        P.inc("vector", "dve", 1, lambda e: e.reduce_max(
            scb[:, 1:2], r128[0:1, 0:64], axis=AX.X))
        P.wait("vector", "dve")
        P.inc("vector", "dve", 1, lambda e: e.tensor_mul(
            scf[:, 0:1], scb[:, 0:1], scb[:, 1:2]))
        P.wait("vector", "dve")
        P.inc("vector", "dve", 1, lambda e: e.tensor_scalar(
            scf[:, 1:2], scf[:, 0:1], -64.0, None, ALU.mult))
        P.wait("tensor", "dve")
        P.inc("tensor", "pe", 1, P.mm(self.PB[:, 2048 + 512:2048 + 513],
                                      self.c_ones_row_f, scf[:, 1:2]))
        P.wait("vector", "pe")
        P.inc("vector", "dve", 1, lambda e: e.tensor_copy(
            self.c_negc, self.PB[:, 2048 + 512:2048 + 513]))
        self.negc_ready = P.ck("dve")

        self.p1buf = [v(O["P1"] + b * 8 * KB, F32, [2048]) for b in range(2)]
        self.wfc_acc = v(M + 4 * KB, F32, [1])
        self.wpj_acc = v(M + 4 * KB + 64, F32, [1])
        self.p1tmp = v(M + 4 * KB + 128, F32, [1])
        self.p1_tiles = []
        for kc in range(NKC):
            for half in range(2):
                self.p1_tiles.append(
                    (din["wfcT"][kc * 128:(kc + 1) * 128,
                                 half * 2048:(half + 1) * 2048], 0))
        for c4 in range(16):
            self.p1_tiles.append(
                (din["wpjT"][c4 * 256:(c4 + 1) * 256, :]
                 .rearrange("(a p) o -> p a o", p=128), 1))
        self.p1_dma_ck = {}
        self.p1_red_ck = {}
        self.p1_dma(0)
        self.p1_dma(1)

    def p1_dma(self, i):
        P = self
        if i >= len(self.p1_tiles):
            return
        if i >= 2:
            P.wait("sync", "dve", self.p1_red_ck[i - 2])
        src, _ = self.p1_tiles[i]
        dst = self.p1buf[i % 2]
        if len(src.shape) == 3:
            dst = dst.rearrange("p (a o) -> p a o", a=src.shape[1])
        P.inc("sync", "dwf", 16, lambda e, i=i, src=src, dst=dst: e.dma_start(
            dst, src))
        self.p1_dma_ck[i] = P.ck("dwf")

    def p1_reduce(self, i):
        P = self
        if i < 0 or i >= len(self.p1_tiles):
            return
        _, acc = self.p1_tiles[i]
        accv = self.wfc_acc if acc == 0 else self.wpj_acc
        first = (i == 0) or (acc == 1 and self.p1_tiles[i - 1][1] == 0)
        P.wait("vector", "dwf", self.p1_dma_ck[i])
        if first:
            P.inc("vector", "dve", 1, lambda e, i=i, accv=accv: e.reduce_max(
                accv, self.p1buf[i % 2], axis=AX.X, apply_absolute_value=True))
        else:
            P.inc("vector", "dve", 1, lambda e, i=i: e.reduce_max(
                self.p1tmp, self.p1buf[i % 2], axis=AX.X,
                apply_absolute_value=True))
            P.wait("vector", "dve")
            P.inc("vector", "dve", 1, lambda e, accv=accv: e.tensor_max(
                accv, accv, self.p1tmp))
        self.p1_red_ck[i] = P.ck("dve")

    def phase_B(self):
        P, v, O = self, self.view, self.O
        rn_row = v(O["RN"], F32, [T], nparts=1)
        rn_bf = v(O["RN"] + 8 * KB, BF16, [T], nparts=1)
        rnB = v(O["RN"] + 12 * KB, F32, [T], nparts=64)
        attT = [v(O["ATT"] + b * 2 * KB, BF16, [1024]) for b in range(2)]
        sc_ps = [self.PB[:, 0:1024], self.PB[:, 1024:2048]]
        yext = self.PB[:, 2048:4096]

        def q_mov(qt, par, cg):
            base = 64 * par
            return self.qT[base:base + 64, cg * 4:(cg + 1) * 4,
                           qt * 128:(qt + 1) * 128]

        exp_ck = {}
        av_ck = {}
        mask_ck = {}
        yev_ck = 0
        p1_next = 2
        j = 0
        pend = []

        def emit_av(item):
            jj, qt, sc, hg, buf = item
            P.wait("tensor", "act", exp_ck[jj])
            if sc == qt:
                P.wait("tensor", "pool", mask_ck[jj])
            P.inc("tensor", "pe", 1, P.mm(
                yext[0:65, hg * 1024: hg * 1024 + 512],
                self.vext[:, sc, :], attT[buf][:, 0:512],
                start=(sc == 0), stop=(sc == qt)))
            av_ck[jj] = P.inc("tensor", "pe", 1, P.mm(
                yext[0:65, hg * 1024 + 512: hg * 1024 + 1024],
                self.vext[:, sc, :], attT[buf][:, 512:1024],
                start=(sc == 0), stop=(sc == qt)))

        for qt in range(NT):
            for sc in range(qt + 1):
                for hg in range(2):
                    buf = j % 2
                    if j >= 2:
                        P.wait("tensor", "act", exp_ck[j - 2])
                    if sc == 0 and hg == 0 and qt > 0:
                        P.wait("tensor", "dve", yev_ck)
                    lhsK = self.kvT[0:64, sc * 128:(sc + 1) * 128]
                    P.inc("tensor", "pe", 1, P.mm(
                        sc_ps[buf][:, 0:512], lhsK, q_mov(qt, 0, hg),
                        tp=(0, 0)))
                    qk_ck = P.inc("tensor", "pe", 1, P.mm(
                        sc_ps[buf][:, 512:1024],
                        self.k2[:, sc * 128:(sc + 1) * 128],
                        q_mov(qt, 1, hg), tp=(64, 0)))
                    if pend:
                        emit_av(pend.pop(0))
                    P.wait("scalar", "pe", qk_ck)
                    if j >= 2:
                        P.wait("scalar", "pe", av_ck.get(j - 2, 0))
                    if j == 0:
                        P.wait("scalar", "dve", self.negc_ready)
                    P.inc("scalar", "act", 1, lambda e, buf=buf: e.activation(
                        attT[buf], sc_ps[buf], AF.Exp,
                        bias=self.c_negc[:, 0:1]))
                    exp_ck[j] = P.ck("act")
                    if sc == qt:
                        P.wait("gpsimd", "act", exp_ck[j])
                        for jj in range(8):
                            P.inc("gpsimd", "pool", 1,
                                  lambda e, buf=buf, jj=jj: e.tensor_mul(
                                      attT[buf][:, jj * 128:(jj + 1) * 128],
                                      attT[buf][:, jj * 128:(jj + 1) * 128],
                                      self.c_tril))
                        mask_ck[j] = P.ck("pool")
                    pend.append((j, qt, sc, hg, buf))
                    j += 1
            while pend:
                emit_av(pend.pop(0))
            P.wait("scalar", "pe", av_ck[j - 1])
            P.inc("scalar", "act", 1, lambda e: e.activation(
                rn_row, yext[64:65, :], AF.Ln))
            P.wait("scalar", "act")
            P.inc("scalar", "act", 1, lambda e: e.activation(
                rn_bf, rn_row, AF.Exp, scale=-1.0))
            P.wait("tensor", "act")
            for q4 in range(4):
                reg = sc_ps[q4 // 2]
                P.inc("tensor", "pe", 1, P.mm(
                    reg[0:64, (q4 % 2) * 512:(q4 % 2) * 512 + 512],
                    self.c_ones_row[0:1, 0:64],
                    rn_bf[0:1, q4 * 512:(q4 + 1) * 512]))
            P.wait("vector", "pe")
            for q4 in range(4):
                reg = sc_ps[q4 // 2]
                P.inc("vector", "dve", 1,
                      lambda e, q4=q4, reg=reg: e.tensor_copy(
                          rnB[:, q4 * 512:(q4 + 1) * 512],
                          reg[0:64, (q4 % 2) * 512:(q4 % 2) * 512 + 512]))
            P.wait("vector", "dve")
            for par in range(2):
                for r in range(2):
                    cols0 = r * 1024 + par * 512
                    dst = self.yT[par * 64:par * 64 + 64, r * 4:(r + 1) * 4,
                                  qt * 128:(qt + 1) * 128]
                    P.inc("vector", "dve", 1,
                          lambda e, dst=dst, cols0=cols0: e.tensor_mul(
                              dst, yext[0:64, cols0:cols0 + 512],
                              rnB[:, cols0:cols0 + 512]))
            if self.debug and qt == 0:
                dbg = self.dbg
                sum0 = v(O["MISC"] + 5 * KB, F32, [T], nparts=1) if False else                     self.view(O["RN"] + 8 * KB, BF16, [T], nparts=1)
                s_tmp = self.view(O["MISC"], F32, [T], nparts=1)
                P.inc("vector", "dve", 1, lambda e, s_tmp=s_tmp: e.tensor_copy(
                    s_tmp, yext[64:65, :]))
                yx_tmp = self.view(O["QT"], BF16, [NKC, T])
                yx = self.view(56 * KB, F32, [T], nparts=64)
                P.inc("vector", "dve", 1, lambda e, yx=yx: e.tensor_copy(
                    yx, yext[0:64, :]))
                dd = P.ck("dve")
                P.wait("sync", "dve", dd)
                P.inc("sync", "dout", 16, lambda e, s_tmp=s_tmp: e.dma_start(
                    dbg["d_sum0"], s_tmp))
                P.inc("sync", "dout", 16, lambda e: e.dma_start(
                    dbg["d_rn0"], rn_row))
                P.inc("sync", "dout", 16, lambda e: e.dma_start(
                    dbg["d_rnB0"], rnB))
                P.inc("sync", "dout", 16, lambda e, yx=yx: e.dma_start(
                    dbg["d_yex0"], yx))
                P.inc("sync", "dout", 16, lambda e: e.dma_start(
                    dbg["d_pre0"], self.view(64 * KB, F32, [T], nparts=1)))
                P.inc("sync", "dout", 16, lambda e: e.dma_start(
                    dbg["d_post0"], self.view(72 * KB, F32, [T], nparts=1)))
                s_rnbf = self.view(88 * KB, F32, [T], nparts=1)
                P.inc("vector", "dve", 1, lambda e, s_rnbf=s_rnbf: e.tensor_copy(
                    s_rnbf, self.view(self.O["RN"] + 8 * KB, BF16, [T], nparts=1)))
                dd2 = P.ck("dve")
                P.wait("sync", "dve", dd2)
                P.inc("sync", "dout", 16, lambda e, s_rnbf=s_rnbf: e.dma_start(
                    dbg["d_rnbf0"], s_rnbf))
                P.wait("vector", "dout")
            yev_ck = P.ck("dve")
            for _ in range(2):
                self.p1_reduce(p1_next - 2)
                self.p1_dma(p1_next)
                p1_next += 1
        while p1_next - 2 < len(self.p1_tiles):
            self.p1_reduce(p1_next - 2)
            self.p1_dma(p1_next)
            p1_next += 1

        M = O["MISC"]
        sfs = v(M + 5 * KB, F32, [16], nparts=1)
        P.wait("sync", "dve")
        P.inc("sync", "dwf", 16, lambda e: e.dma_start(
            self.scr_acc[0, :], self.wfc_acc))
        P.inc("sync", "dwf", 16, lambda e: e.dma_start(
            self.scr_acc[1, :], self.wpj_acc))
        P.wait("sync", "dwf")
        accrow = v(M + 6 * KB, F32, [2, 128], nparts=1)
        P.inc("sync", "dwf", 16, lambda e: e.dma_start(
            accrow, self.scr_acc.rearrange("a p -> (a p)")))
        acc_row_ck = P.ck("dwf")
        for wi, (rsc, ssc) in enumerate([
                (self.c_rscale_fc, self.c_scale_fc),
                (self.c_rscale_pj, self.c_scale_pj)]):
            P.wait("vector", "dwf", acc_row_ck)
            P.inc("vector", "dve", 1, lambda e, wi=wi: e.reduce_max(
                sfs[:, wi * 4:wi * 4 + 1], accrow[:, wi, :], axis=AX.X))
            P.wait("vector", "dve")
            P.inc("vector", "dve", 1, lambda e, wi=wi: e.reciprocal(
                sfs[:, wi * 4 + 1:wi * 4 + 2], sfs[:, wi * 4:wi * 4 + 1]))
            P.wait("vector", "dve")
            P.inc("vector", "dve", 1, lambda e, wi=wi: e.tensor_scalar(
                sfs[:, wi * 4 + 2:wi * 4 + 3], sfs[:, wi * 4 + 1:wi * 4 + 2],
                127.0, None, ALU.mult))
            P.inc("vector", "dve", 1, lambda e, wi=wi: e.tensor_scalar(
                sfs[:, wi * 4 + 3:wi * 4 + 4], sfs[:, wi * 4:wi * 4 + 1],
                1.0 / 127.0, None, ALU.mult))
            for si, dstv in ((2, rsc), (3, ssc)):
                P.wait("tensor", "dve")
                P.inc("tensor", "pe", 1, P.mm(
                    self.PB[:, 128:129], self.c_ones_row_f,
                    sfs[:, wi * 4 + si:wi * 4 + si + 1]))
                P.wait("vector", "pe")
                P.inc("vector", "dve", 1, lambda e, dstv=dstv: e.tensor_copy(
                    dstv, self.PB[:, 128:129]))
            nsc = self.c_nsc_fc if wi == 0 else self.c_nsc_pj
            nb = self.c_nb_fc if wi == 0 else self.c_nb_pj
            P.wait("vector", "dve")
            P.inc("vector", "dve", 1, lambda e, ssc=ssc, nsc=nsc: e.tensor_scalar(
                nsc, ssc, -1.0, None, ALU.mult))
            P.inc("vector", "dve", 1, lambda e, ssc=ssc, nb=nb: e.tensor_scalar(
                nb, ssc, -256.0, None, ALU.mult))
        self.scales_ready = P.ck("dve")

        if self.debug:
            P.wait("sync", "act")
            P.wait("sync", "dve")
            dbg = self.dbg
            P.inc("sync", "dout", 16, lambda e: e.dma_start(
                dbg["d_latentT"], v(O["LATT"], BF16, [NLC * T])))
            P.inc("sync", "dout", 16, lambda e: e.dma_start(
                dbg["d_qT"], v(O["QT"], BF16, [NKC * T])))
            P.inc("sync", "dout", 16, lambda e: e.dma_start(
                dbg["d_kvT"], self.kvT))
            P.inc("sync", "dout", 16, lambda e: e.dma_start(
                dbg["d_negc"], self.c_negc))
            P.inc("sync", "dout", 16, lambda e: e.dma_start(
                dbg["d_yT"], v(O["YT"], BF16, [NKC * T])))

    def phase_C(self):
        P, v, O, din = self, self.view, self.O, self.din
        x1T = v(O["X1T"], F32, [NKC, T])
        xn2T = v(O["XN2"], BF16, [NKC, T])
        w_cproj = v(O["WCPROJ"], BF16, [NKC, 1024])
        xres = [v(O["XRES"] + b * 2 * KB, F32, [SEG]) for b in range(4)]
        self.x1T, self.xn2T = x1T, xn2T

        P.wait("gpsimd", "dve", self.scales_ready)
        P.inc("gpsimd", "dw", 16, lambda e: e.dma_start(
            w_cproj, din["wcproj"].rearrange("(kc p) o -> p kc o", p=128)))
        PP = O["P2"]
        afcb = v(PP + 8 * KB, BF16, [D], nparts=16)
        bfcb = v(PP + 10 * KB, BF16, [FF], nparts=16)
        apjb = v(PP + 18 * KB, BF16, [FF], nparts=16)
        bpjb = v(PP + 26 * KB, BF16, [D], nparts=16)
        P.inc("gpsimd", "dw", 16, lambda e: e.dma_start(afcb, din["afc"]))
        P.inc("gpsimd", "dw", 16, lambda e: e.dma_start(bfcb, din["bfcT"]))
        P.inc("gpsimd", "dw", 16, lambda e: e.dma_start(apjb, din["apj"]))
        P.inc("gpsimd", "dw", 16, lambda e: e.dma_start(bpjb, din["bpjT"]))
        P.wait("vector", "dw")
        P.inc("vector", "dve", 1, lambda e: e.tensor_scalar(
            afcb, afcb, 2.0, None, ALU.mult))
        P.inc("vector", "dve", 1, lambda e: e.tensor_scalar(
            apjb, apjb, 2.0, None, ALU.mult))
        ab_ready = P.ck("dve")

        xres_ck = {}
        for oc in range(NKC):
            for seg in range(NSEG):
                i = oc * NSEG + seg
                if i >= 4:
                    P.wait("sync", "dve", xres_ck[i - 4])
                P.inc("sync", "dres", 16, lambda e, oc=oc, seg=seg, i=i:
                      e.dma_start(xres[i % 4],
                                  din["xT"][oc * 128:(oc + 1) * 128,
                                            seg * SEG:(seg + 1) * SEG]))
            if oc == 0:
                P.wait("tensor", "dw")
            else:
                P.wait("tensor", "dve")
            for kc in range(NKC):
                for seg in range(NSEG):
                    P.inc("tensor", "pe", 1, P.mm(
                        self.pbank(seg),
                        w_cproj[:, kc, oc * 128:(oc + 1) * 128],
                        self.yT[:, kc, seg * SEG:(seg + 1) * SEG],
                        start=(kc == 0), stop=(kc == NKC - 1)))
            P.wait("vector", "pe")
            P.wait("vector", "dres")
            for seg in range(NSEG):
                i = oc * NSEG + seg
                P.inc("vector", "dve", 1,
                      lambda e, oc=oc, seg=seg, i=i: e.tensor_add(
                          x1T[:, oc, seg * SEG:(seg + 1) * SEG],
                          self.pbank(seg), xres[i % 4]))
                xres_ck[i] = P.ck("dve")

        x1_done = P.ck("dve")

        def ln2_wait():
            P.wait("scalar", "dve", x1_done)

        for seg in range(NSEG):
            src = [x1T[:, kc, seg * SEG:(seg + 1) * SEG] for kc in range(NKC)]
            dst = [xn2T[:, kc, seg * SEG:(seg + 1) * SEG] for kc in range(NKC)]
            self.emit_layernorm_seg(src, O["STATS2"], dst,
                                    ln2_wait if seg == 0 else None)
        self.ln2_done_dve = P.ck("dve")

        p2w = [v(PP + b * 2 * KB, F32, [SEG]) for b in range(2)]
        p2i = v(PP + 4 * KB, I32, [SEG])
        p2u = v(PP + 4 * KB, F32, [SEG])
        p2f = v(PP + 6 * KB, F32, [SEG])
        p2r = v(PP + 30 * KB, F32, [SEG])
        p2s = [v(PP + 28 * KB + b * KB, BF16, [SEG]) for b in range(2)]
        tiles = []
        for kc in range(NKC):
            for cb in range(8):
                tiles.append(("fc", kc, cb))
        for kc2 in range(NFC):
            for cb2 in range(2):
                tiles.append(("pj", kc2, cb2))
        p2_dma_ck = {}
        p2_done = {}
        for i, t in enumerate(tiles):
            kind, kc, cb = t
            src = (din["wfcT"] if kind == "fc" else din["wpjT"])[
                kc * 128:(kc + 1) * 128, cb * 512:(cb + 1) * 512]
            dstd = (self.scr_fc if kind == "fc" else self.scr_pj)[cb, :, kc, :]
            if i >= 2:
                P.wait("sync", "dve", p2_done[i - 2])
            P.inc("sync", "dwf", 16, lambda e, i=i, src=src: e.dma_start(
                p2w[i % 2], src))
            p2_dma_ck[i] = P.ck("dwf")
            ba = self.PB[:, (4 + i % 2) * 512:(4 + i % 2) * 512 + 512]
            if i == 0:
                P.wait("tensor", "dve", ab_ready)
            if i >= 2:
                P.wait("tensor", "dve", p2_done[i - 2])
            if kind == "fc":
                lhs, rhs = afcb[:, kc * 128:(kc + 1) * 128], \
                    bfcb[:, cb * 512:(cb + 1) * 512]
            else:
                lhs, rhs = apjb[:, kc * 128:(kc + 1) * 128], \
                    bpjb[:, cb * 512:(cb + 1) * 512]
            ba_ck = P.inc("tensor", "pe", 1, P.mm(ba, lhs, rhs))
            rsc = self.c_rscale_fc if kind == "fc" else self.c_rscale_pj
            nsc = self.c_nsc_fc if kind == "fc" else self.c_nsc_pj
            nb = self.c_nb_fc if kind == "fc" else self.c_nb_pj
            P.wait("vector", "dwf", p2_dma_ck[i])
            if i == 0:
                P.wait("vector", "dve", self.scales_ready)
            P.inc("vector", "dve", 1, lambda e, i=i, rsc=rsc: e.tensor_scalar(
                p2r, p2w[i % 2], rsc[:, 0:1], 256.5, ALU.mult, ALU.add))
            P.wait("vector", "dve")
            P.inc("vector", "dve", 1, lambda e: e.tensor_copy(p2i, p2r))
            P.wait("vector", "dve")
            P.inc("vector", "dve", 1, lambda e: e.tensor_copy(p2f, p2i))
            P.wait("vector", "dve")
            P.inc("vector", "dve", 1, lambda e: e.tensor_sub(p2r, p2r, p2f))
            P.wait("vector", "dve")
            P.inc("vector", "dve", 1, lambda e: e.scalar_tensor_tensor(
                p2u, p2r, 0.0, p2f, ALU.is_lt, ALU.subtract))
            P.wait("vector", "dve")
            P.inc("vector", "dve", 1, lambda e, nsc=nsc: e.tensor_scalar(
                p2r, p2u, 256.0, nsc[:, 0:1], ALU.add, ALU.mult))
            P.wait("vector", "pe", ba_ck)
            P.wait("vector", "dve")
            if i >= 2:
                P.wait("vector", "dscr", 16 * (i - 1))
            P.inc("vector", "dve", 1, lambda e, i=i, ba=ba: e.tensor_add(
                p2s[i % 2], p2r, ba))
            p2_done[i] = P.ck("dve")
            P.wait("gpsimd", "dve", p2_done[i])
            P.inc("gpsimd", "dscr", 16, lambda e, i=i, dstd=dstd: e.dma_start(
                dstd, p2s[i % 2]))

        if self.debug:
            P.wait("sync", "dve")
            dbg = self.dbg
            P.inc("sync", "dout", 16, lambda e: e.dma_start(
                dbg["d_x1T"], v(O["X1T"], F32, [NKC * T])))
            P.inc("sync", "dout", 16, lambda e: e.dma_start(
                dbg["d_xn2T"], v(O["XN2"], BF16, [NKC * T])))
            P.wait("sync", "dscr")
            for cb in range(8):
                P.inc("sync", "dout", 16, lambda e, cb=cb: e.dma_start(
                    dbg["d_wfc"][cb], self.scr_fc[cb]))
            for cb in range(2):
                P.inc("sync", "dout", 16, lambda e, cb=cb: e.dma_start(
                    dbg["d_wpj"][cb], self.scr_pj[cb]))

    def phase_D(self):
        P, v, O = self, self.view, self.O
        x1T, xn2T = self.x1T, self.xn2T
        hT = v(O["HT"], BF16, [NFC, SEG])
        wfc_t = [v(O["WFC_D"] + b * 8 * KB, BF16, [NKC, 512]) for b in range(2)]
        wpj_t = v(O["WPJ_D"], BF16, [16, 512])

        scr_written = P.ck("dscr")
        gelu_ck = {}
        fc_free_ck = {}
        prev_evac_dve = 0
        for gseg in range(NSEG):
            gcols = slice(gseg * SEG, (gseg + 1) * SEG)
            for ocg in range(8):
                i = gseg * 8 + ocg
                if i == 0:
                    P.wait("sync", "dscr", scr_written)
                if i >= 2:
                    P.wait("sync", "pe", fc_free_ck[i - 2])
                P.inc("sync", "dffn", 16, lambda e, ocg=ocg, i=i: e.dma_start(
                    wfc_t[i % 2], self.scr_fc[ocg]))
                dck = P.ck("dffn")
                P.wait("tensor", "dffn", dck)
                if i == 0:
                    P.wait("tensor", "dve", self.ln2_done_dve)
                for jj in range(4):
                    oc = ocg * 4 + jj
                    gi = i * 4 + jj
                    pb = self.pbank(jj % 2)
                    if gi >= 2:
                        P.wait("tensor", "act", gelu_ck[gi - 2])
                    for kc in range(NKC):
                        P.inc("tensor", "pe", 1, P.mm(
                            pb, wfc_t[i % 2][:, kc, jj * 128:(jj + 1) * 128],
                            xn2T[:, kc, gcols],
                            start=(kc == 0), stop=(kc == NKC - 1)))
                    mmck = P.ck("pe")
                    P.wait("scalar", "pe", mmck)
                    P.inc("scalar", "act", 1,
                          lambda e, oc=oc, pb=pb: e.activation(
                              hT[:, oc, :], pb, AF.Gelu,
                              bias=self.c_bfc[:, oc:oc + 1]))
                    gelu_ck[gi] = P.ck("act")
                fc_free_ck[i] = P.ck("pe")
            P.wait("tensor", "act", P.ck("act"))
            for cb2 in range(2):
                if prev_evac_dve:
                    P.wait("tensor", "dve", prev_evac_dve)
                for kh in range(2):
                    P.wait("sync", "pe")
                    P.inc("sync", "dffn", 16,
                          lambda e, cb2=cb2, kh=kh: e.dma_start(
                              wpj_t, self.scr_pj[cb2, :, kh * 16:(kh + 1) * 16, :]))
                    P.wait("tensor", "dffn")
                    for jj in range(4):
                        pb = self.PB[:, (4 + jj) * 512:(4 + jj) * 512 + 512]
                        for k16 in range(16):
                            kc2 = kh * 16 + k16
                            P.inc("tensor", "pe", 1, P.mm(
                                pb, wpj_t[:, k16, jj * 128:(jj + 1) * 128],
                                hT[:, kc2, :],
                                start=(kh == 0 and k16 == 0),
                                stop=(kh == 1 and k16 == 15)))
                P.wait("vector", "pe")
                for jj in range(4):
                    oc2 = cb2 * 4 + jj
                    pb = self.PB[:, (4 + jj) * 512:(4 + jj) * 512 + 512]
                    P.inc("vector", "dve", 1,
                          lambda e, oc2=oc2, pb=pb, gcols=gcols:
                          e.scalar_tensor_tensor(
                              x1T[:, oc2, gcols], pb,
                              self.c_bpj[:, oc2:oc2 + 1],
                              x1T[:, oc2, gcols], ALU.add, ALU.add))
                    dveck = P.ck("dve")
                    P.wait("sync", "dve", dveck)
                    P.inc("sync", "dout", 16,
                          lambda e, oc2=oc2, gcols=gcols: e.dma_start(
                              self.outT[oc2 * 128:(oc2 + 1) * 128, gcols],
                              x1T[:, oc2, gcols]))
                prev_evac_dve = P.ck("dve")

    def emit_debug_taps_final(self):
        pass


def build_program(debug=False):
    b = Builder(debug=debug)
    nc = b.build()
    return b, nc


def kernel(L0_attn_x, L0_attn_Wc_down, L0_attn_Wq_up, L0_attn_Wkv_up,
           L0_attn_Wc_proj, L0_ffn_W_fc, L0_ffn_b_fc, L0_ffn_A_fc,
           L0_ffn_B_fc, L0_ffn_W_proj, L0_ffn_b_proj, L0_ffn_A_proj,
           L0_ffn_B_proj):
    b, nc = build_program(debug=False)

    bf = ml_dtypes.bfloat16
    shared = {
        "wcd": np.ascontiguousarray(L0_attn_Wc_down, dtype=np.float32),
        "wqup": np.ascontiguousarray(L0_attn_Wq_up, dtype=np.float32),
        "wkv": np.ascontiguousarray(L0_attn_Wkv_up, dtype=np.float32),
        "wcproj": np.ascontiguousarray(L0_attn_Wc_proj, dtype=np.float32),
        "wfcT": np.ascontiguousarray(np.asarray(L0_ffn_W_fc).T, dtype=np.float32),
        "wpjT": np.ascontiguousarray(np.asarray(L0_ffn_W_proj).T, dtype=np.float32),
        "afc": np.ascontiguousarray(L0_ffn_A_fc, dtype=np.float32),
        "bfcT": np.ascontiguousarray(np.asarray(L0_ffn_B_fc).T, dtype=np.float32),
        "apj": np.ascontiguousarray(L0_ffn_A_proj, dtype=np.float32),
        "bpjT": np.ascontiguousarray(np.asarray(L0_ffn_B_proj).T, dtype=np.float32),
        "bfc2d": np.ascontiguousarray(
            np.asarray(L0_ffn_b_fc).reshape(NFC, 128).T, dtype=np.float32),
        "bpj2d": np.ascontiguousarray(
            np.asarray(L0_ffn_b_proj).reshape(NKC, 128).T, dtype=np.float32),
        "ident": np.eye(128, dtype=bf),
        "tril": (np.tril(np.ones((128, 128), dtype=np.float32)).T
                 ).astype(bf),
    }
    x = np.asarray(L0_attn_x)
    in_maps = []
    for c in range(8):
        m = dict(shared)
        m["xT"] = np.ascontiguousarray(x[c].T, dtype=np.float32)
        in_maps.append(m)
    res = run_bass_kernel_spmd(nc, in_maps, core_ids=list(range(8)))
    outs = []
    for c in range(8):
        o = np.asarray(res.results[c]["outT"])
        outs.append(np.ascontiguousarray(o.T))
    return np.stack(outs, axis=0)
